# version 01 (final)
# baseline (speedup 1.0000x reference)
import numpy as np
import concourse.bass as bass
import concourse.mybir as mybir
from concourse.bass_utils import run_bass_kernel_spmd

F32 = mybir.dt.float32
BF16 = mybir.dt.bfloat16
I32 = mybir.dt.int32
AF = mybir.ActivationFunctionType
ALU = mybir.AluOpType
AXX = mybir.AxisListType.X

N_CORES = 8
SEQS = 2
T = 2048
D = 1024
HID = 4096
EPS = 1e-5
K9 = 9
TC = 512
NTC = T // TC
QC = 256
NQC = T // QC
NKT = T // 128
VST = 160

N_SUB = 4
PRO_LEVEL = 5
ATA_LEVEL = 5
STATS_LEVEL = 4
DBG = True
FFN_CUT = 0
FFN2_CUT = 0

ENGINES = ("tensor", "vector", "scalar", "gpsimd", "sync")


class Prog:
    def __init__(self, nc):
        self.nc = nc
        self.q = {e: [] for e in ENGINES}
        self.segments = []
        self.sems = {}
        self.cnt = {}
        self.lastw = {e: {} for e in ENGINES}

    def defsem(self, name, handle):
        self.sems[name] = handle
        self.cnt[name] = 0

    AUTO = {"vector": "Ev", "scalar": "Es", "gpsimd": "Eg"}

    def op(self, eng, fn, wait=(), inc=()):
        auto = self.AUTO.get(eng)
        if auto is not None and fn is not None and self.cnt.get(auto, 0) > 0:
            wait = tuple(wait) + ((auto, self.cnt[auto]),)
        wlist = []
        for w in wait:
            if w is None:
                continue
            s, v = w
            if self.lastw[eng].get(s, -1) >= v:
                continue
            self.lastw[eng][s] = v
            wlist.append((s, v))
        toks = []
        for (s, k) in inc:
            self.cnt[s] += k
            toks.append((s, self.cnt[s]))
        self.q[eng].append((tuple(wlist), fn, tuple(inc)))
        return toks[0] if len(toks) == 1 else (toks or None)

    def barrier(self):
        self.op("sync", None, wait=[("Ds", self.cnt["Ds"])])
        self.op("gpsimd", None, wait=[("Dg", self.cnt["Dg"]),
                                      ("Eg", self.cnt["Eg"])])
        self.op("vector", None, wait=[("Ev", self.cnt["Ev"])])
        self.op("scalar", None, wait=[("Es", self.cnt["Es"])])
        if any(self.q[e] for e in ENGINES):
            self.segments.append(self.q)
            self.q = {e: [] for e in ENGINES}
        snap = dict(self.cnt)
        for e in ENGINES:
            self.lastw[e].update(snap)
        return None

    def replay_all(self):
        self.barrier()
        for seg in self.segments:
            with self.nc.Block() as block:
                def body(name, seg=seg):
                    def run(E):
                        for wlist, fn, incs in seg[name]:
                            for (s, v) in wlist:
                                E.wait_ge(self.sems[s], v)
                            if fn is not None:
                                inst = fn(E)
                                for (s, k) in incs:
                                    inst.then_inc(self.sems[s], k)
                    return run
                block.tensor(body("tensor"))
                block.vector(body("vector"))
                block.scalar(body("scalar"))
                block.gpsimd(body("gpsimd"))
                block.sync(body("sync"))


def _host_consts(conv_weight):
    import ml_dtypes
    c = {}
    c["onesf"] = np.ones((128, 128), np.float32)
    cwl = np.zeros((K9, 8, 128, 18), np.float32)
    for tau in range(K9):
        for hp in range(8):
            for eta in range(2):
                h = hp * 2 + eta
                cwl[tau, hp, eta * 64:(eta + 1) * 64, eta * 9:(eta + 1) * 9] = \
                    conv_weight[h * 9:(h + 1) * 9, :, tau].T
    c["cwl"] = np.ascontiguousarray(cwl)
    bsel2 = np.zeros((128, K9 * 128), ml_dtypes.bfloat16)
    for k in range(4):
        for tau in range(K9):
            for eta in range(2):
                bsel2[32 * k + eta * 9 + tau,
                      tau * 128 + eta * 64: tau * 128 + (eta + 1) * 64] = 1
    c["bsel2"] = bsel2
    g2 = np.zeros((128, 8), ml_dtypes.bfloat16)
    for k in range(4):
        for eta in range(2):
            for j in range(K9):
                g2[32 * k + eta * 9 + j, 2 * k + eta] = 1
    c["g2full"] = g2
    brec = np.zeros((8, 512), ml_dtypes.bfloat16)
    for k in range(4):
        for eta in range(2):
            brec[2 * k + eta, k * 128 + eta * 64: k * 128 + (eta + 1) * 64] = 1
    c["brec"] = brec
    selfc = np.zeros((128, 8 * 128), ml_dtypes.bfloat16)
    for hp in range(8):
        selfc[64 + 2 * hp, hp * 128: hp * 128 + 64] = 1
        selfc[33 + 2 * hp, hp * 128 + 64: hp * 128 + 128] = 1
    c["selfc"] = selfc
    return c


def build_nc():
    nc = bass.Bass("TRN2", target_bir_lowering=False,
                   detect_race_conditions=True)
    ap = lambda t: t.ap()
    x_in = ap(nc.dram_tensor("x_in", [SEQS, D, T], F32, kind="ExternalInput"))
    y_out = ap(nc.dram_tensor("y", [SEQS, D, T], F32, kind="ExternalOutput"))
    dbg_out = ap(nc.dram_tensor("dbg", [8, 128, 512], F32, kind="ExternalOutput"))
    dbg_xs = ap(nc.dram_tensor("dbg_xs", [128, 8 * TC], BF16, kind="ExternalOutput"))
    dbg_h = ap(nc.dram_tensor("dbg_h", [128, 32 * TC], BF16, kind="ExternalOutput"))
    dbg_r = ap(nc.dram_tensor("dbg_r", [128, T], F32, kind="ExternalOutput"))
    dbg2 = ap(nc.dram_tensor("dbg2", [8, 128, 512], F32, kind="ExternalOutput"))
    dbg_w = ap(nc.dram_tensor("dbg_w", [128, 8 * 512], BF16, kind="ExternalOutput"))
    dbg_p0 = ap(nc.dram_tensor("dbg_p0", [128, 512], F32, kind="ExternalOutput"))
    w1t = ap(nc.dram_tensor("w1t", [D, HID], F32, kind="ExternalInput"))
    w2t = ap(nc.dram_tensor("w2t", [HID, D], F32, kind="ExternalInput"))
    wq_in = ap(nc.dram_tensor("wq_in", [D, D], F32, kind="ExternalInput"))
    wk_in = ap(nc.dram_tensor("wk_in", [D, D], F32, kind="ExternalInput"))
    wv_in = ap(nc.dram_tensor("wv_in", [D, D], F32, kind="ExternalInput"))
    wot_in = ap(nc.dram_tensor("wot_in", [D, D], F32, kind="ExternalInput"))
    cwl_in = ap(nc.dram_tensor("cwl", [K9, 8, 128, 18], F32, kind="ExternalInput"))
    ln1g_in = ap(nc.dram_tensor("ln1g", [D], F32, kind="ExternalInput"))
    ln1b_in = ap(nc.dram_tensor("ln1b", [D], F32, kind="ExternalInput"))
    b1f_in = ap(nc.dram_tensor("b1f", [HID], F32, kind="ExternalInput"))
    ln2g_in = ap(nc.dram_tensor("ln2g", [HID], F32, kind="ExternalInput"))
    ln2b_in = ap(nc.dram_tensor("ln2b", [HID], F32, kind="ExternalInput"))
    b2f_in = ap(nc.dram_tensor("b2f", [D], F32, kind="ExternalInput"))
    onesf_in = ap(nc.dram_tensor("onesf", [128, 128], F32, kind="ExternalInput"))
    bsel2_in = ap(nc.dram_tensor("bsel2", [128, K9 * 128], BF16, kind="ExternalInput"))
    g2full_in = ap(nc.dram_tensor("g2full", [128, 8], BF16, kind="ExternalInput"))
    brec_in = ap(nc.dram_tensor("brec", [8, 512], BF16, kind="ExternalInput"))
    selfc_in = ap(nc.dram_tensor("selfc", [128, 8 * 128], BF16, kind="ExternalInput"))
    w1g_d = ap(nc.dram_tensor("w1g_d", [8, 128, HID], BF16, kind="Internal"))
    w2g_d = ap(nc.dram_tensor("w2g_d", [32, 128, D], BF16, kind="Internal"))
    wq_d = ap(nc.dram_tensor("wq_d", [8, 128, D], BF16, kind="Internal"))
    wk_d = ap(nc.dram_tensor("wk_d", [8, 128, D], BF16, kind="Internal"))
    wv_d = ap(nc.dram_tensor("wv_d", [8, 128, D], BF16, kind="Internal"))
    wo_d = ap(nc.dram_tensor("wo_d", [8, 128, D], BF16, kind="Internal"))
    cs_d = ap(nc.dram_tensor("cs_d", [HID], F32, kind="Internal"))
    cs2_d = ap(nc.dram_tensor("cs2_d", [HID], F32, kind="Internal"))

    BASE = (nc.sbuf_base + 63) // 64 * 64
    LIMIT = nc.sbuf_top - nc.sbuf_base - 64
    SZ = {F32: 4, BF16: 2, I32: 4}

    def at(off, name, shape, dtype):
        assert off + shape[-1] * SZ[dtype] <= LIMIT, (name, off)
        return nc.alloc_sbuf_tensor_at(name, shape, dtype, offset=BASE + off).ap()

    OFFS = {}

    class Region:
        def __init__(self, base):
            self.off = base

        def a(self, name, shape, dtype):
            self.off = (self.off + 63) // 64 * 64
            t = at(self.off, name, shape, dtype)
            OFFS[name] = self.off
            self.off += shape[-1] * SZ[dtype]
            return t

    P = Region(0)
    xres = P.a("xres", [128, 8 * T], F32)
    r_bc = P.a("r_bc", [128, T], F32)
    nmur_row = P.a("nmur_row", [1, T], BF16)
    onesf = P.a("onesf_sb", [128, 128], F32)
    ones_bf = P.a("ones_bf", [128, 128], BF16)
    g2full = P.a("g2full_sb", [128, 8], BF16)
    bsel2 = P.a("bsel2_sb", [128, K9 * 128], BF16)
    brec = P.a("brec_sb", [8, 512], BF16)
    selfc = P.a("selfc_sb", [128, 8 * 128], BF16)
    cwl_bf = P.a("cwl_bf", [128, K9 * 8 * 18], BF16)
    s1_row = P.a("s1_row", [1, HID], BF16)
    s2_row = P.a("s2_row", [1, D], BF16)
    c1_col = P.a("c1_col", [128, 32], F32)
    c2_col = P.a("c2_col", [128, 8], F32)
    wqsum = P.a("wqsum", [1, D], BF16)
    wksum = P.a("wksum", [1, D], BF16)
    wvsum = P.a("wvsum", [1, D], BF16)
    g1_colf = P.a("g1_colf", [128, 8], F32)
    g2_colf = P.a("g2_colf", [128, 32], F32)
    b1g1_bf = P.a("b1g1_bf", [128, 16], BF16)
    b2g2_bf = P.a("b2g2_bf", [128, 64], BF16)
    r3_bc = P.a("r3_bc", [128, TC], F32)
    nm3_row = P.a("nm3_row", [1, TC], BF16)
    cpool = P.a("cpool", [128, 4], F32)
    PH = (P.off + 63) // 64 * 64

    S = Region(PH)
    t_mu = S.a("t_mu", [128, T], F32)
    t_v = S.a("t_v", [128, T], F32)
    t_x = S.a("t_x", [128, T], F32)
    xb_s = S.a("xb_s", [128, 2 * T], BF16)
    x2_s = S.a("x2_s", [128, 2 * T], BF16)
    STATS_END = S.off

    R = Region(PH)
    PRO = {}
    for nm, sh, dt in [("wblkA", [128, HID], F32), ("wblkB", [128, HID], F32),
                       ("wscr", [128, HID], F32), ("wtp", [128, HID], F32),
                       ("wqbf", [128, HID], BF16), ("wgbf", [128, HID], BF16),
                       ("acc_abs", [128, 1], F32), ("red1", [128, 1], F32),
                       ("scal", [128, 2], F32), ("stage", [2, 512], F32),
                       ("b1f_col", [128, 32], F32), ("b2f_col", [128, 8], F32)]:
        PRO[nm] = R.a("pro_" + nm, sh, dt)

    R = Region(PH)
    FFN = {}
    for nm, sh, dt in [("xs", [128, 8 * TC], BF16), ("h", [128, 32 * TC], BF16),
                       ("wblkA", [128, 4096], BF16), ("wblkB", [128, 4096], BF16),
                       ("h2", [128, 4 * TC], BF16),
                       ("t3a", [128, TC], F32), ("t3b", [128, TC], F32),
                       ("ev", [128, 2 * TC], BF16)]:
        FFN[nm] = R.a("ffn_" + nm, sh, dt)

    PADW = T + 8
    R = Region(PH)
    ATA = {}
    for nm, sh, dt in [("xpad", [128, 8 * PADW], BF16),
                       ("xpad1", [128, 8 * PADW], BF16),
                       ("nmur_f", [128, T], F32),
                       ("xlns", [128, T], F32),
                       ("pa", [128, 512], BF16),
                       ("acc", [128, 4 * 512], BF16),
                       ("prod", [128, 512], BF16),
                       ("rcp_f", [8, 512], F32), ("rcp_b", [8, 512], BF16),
                       ("norm", [128, 512], BF16)]:
        ATA[nm] = R.a("ata_" + nm, sh, dt)

    R = Region(PH)
    ATB = {}
    for nm, sh, dt in [("K", [128, 8 * T], BF16),
                       ("v", [128, NKT * 8 * VST], BF16),
                       ("xs5", [128, 8 * 512], BF16),
                       ("Q", [128, 8 * QC], BF16),
                       ("ptA", [128, 4 * QC], BF16), ("ptB", [128, 4 * QC], BF16),
                       ("ao", [128, 8 * QC], BF16),
                       ("wbufA", [128, 2 * D], BF16), ("wbufB", [128, 2 * D], BF16),
                       ("dens", [128, QC], F32), ("rcpf", [128, QC], F32),
                       ("rcpb", [128, QC], BF16), ("ostg", [128, QC], F32)]:
        ATB[nm] = R.a("atb_" + nm, sh, dt)

    assert OFFS["ata_nmur_f"] >= STATS_END
    assert OFFS["ata_xlns"] >= STATS_END
    assert OFFS["atb_dens"] >= STATS_END

    nc.const_aps.register(F32, EPS, cpool[:, 0:1])

    PS = nc.alloc_psum_tensor("PS", [128, 4096], F32).ap()

    def bank(i, rows=128, w=512, coff=0):
        return PS[0:rows, i * 512 + coff: i * 512 + coff + w]

    import contextlib
    ctx = contextlib.ExitStack()
    p = Prog(nc)
    for snm in ("Et", "Ev", "Es", "Eg", "bar", "Ds", "Dg"):
        p.defsem(snm, ctx.enter_context(nc.semaphore(snm)))

    def dma(eng, out, in_, wait=(), slow=False):
        sem = "Ds" if eng == "sync" else "Dg"
        if slow:
            def fn(E, o_=out, i_=in_):
                with nc.allow_non_contiguous_dma(reason="tiny prologue load"):
                    return E.dma_start(out=o_, in_=i_)
        else:
            fn = (lambda o_, i_: (lambda E: E.dma_start(out=o_, in_=i_)))(out, in_)
        wait = tuple(wait) + ((sem, p.cnt[sem]),)
        return p.op(eng, fn, wait=wait, inc=((sem, 16),))

    def mm(out, lhsT, rhs, start, stop, tp=None, wait=(), mark=False):
        fn = (lambda o_, l_, r_, a_, b_, t_:
              (lambda E: E.matmul(o_, l_, r_, start=a_, stop=b_, tile_position=t_)))(
                  out, lhsT, rhs, start, stop, tp)
        return p.op("tensor", fn, wait=wait, inc=((("Et", 1),) if mark else ()))

    def vop(fn, wait=(), mark=False):
        return p.op("vector", fn, wait=wait, inc=(("Ev", 1),))

    def sop(fn, wait=(), mark=False):
        return p.op("scalar", fn, wait=wait, inc=(("Es", 1),))

    def gop(fn, wait=(), mark=False):
        return p.op("gpsimd", fn, wait=wait, inc=(("Eg", 1),))

    def emit_prologue():
        dma("sync", onesf[:], onesf_in)
        dma("sync", bsel2[:], bsel2_in)
        dma("sync", g2full[:], g2full_in)
        dma("sync", brec[:], brec_in)
        dma("sync", selfc[:], selfc_in)
        dma("gpsimd", cwl_bf[:].rearrange("p (t hp o) -> p t hp o", t=K9, hp=8),
            cwl_in.rearrange("t hp p o -> p t hp o"))
        dma("gpsimd",
            b1g1_bf[:].rearrange("p (dc two) -> p two dc", two=2)[:, 0, :],
            ln1b_in.rearrange("(dc p) -> p dc", p=128), slow=True)
        dma("gpsimd",
            b1g1_bf[:].rearrange("p (dc two) -> p two dc", two=2)[:, 1, :],
            ln1g_in.rearrange("(dc p) -> p dc", p=128), slow=True)
        dma("gpsimd",
            b2g2_bf[:].rearrange("p (jc two) -> p two jc", two=2)[:, 0, :],
            ln2b_in.rearrange("(jc p) -> p jc", p=128), slow=True)
        dma("gpsimd",
            b2g2_bf[:].rearrange("p (jc two) -> p two jc", two=2)[:, 1, :],
            ln2g_in.rearrange("(jc p) -> p jc", p=128), slow=True)
        dma("sync", g1_colf[:], ln1g_in.rearrange("(dc p) -> p dc", p=128), slow=True)
        dma("sync", g2_colf[:], ln2g_in.rearrange("(jc p) -> p jc", p=128), slow=True)
        gop(lambda E: E.memset(ones_bf[:], 1.0))
        gop(lambda E: E.memset(cpool[:, 0:1], EPS))
        p.barrier()
        if PRO_LEVEL < 2:
            return

        def quant(wt_ap, nsup, nsub, bw, dst_fn, src_fn, bvec_col, g_colf, csbanks):
            BW = nsub * bw
            bufs = [PRO["wblkA"], PRO["wblkB"]]

            def ldview(bb):
                v = bufs[bb][:, 0:BW]
                return v if nsub == 1 else v.rearrange("p (c w) -> p c w", c=nsub)
            vop(lambda E: E.memset(PRO["acc_abs"][:], 0.0))
            ptoks = {}
            for b in range(nsup):
                w8 = [ptoks[b - 2]] if b >= 2 else []
                tok = dma("sync", ldview(b % 2), src_fn(b), wait=w8)
                vop((lambda bb: (lambda E: E.reduce_sum(
                    PRO["red1"][:], bufs[bb][:, 0:BW], axis=AXX,
                    apply_absolute_value=True)))(b % 2), wait=[tok])
                ptoks[b] = vop(lambda E: E.tensor_add(
                    PRO["acc_abs"][:], PRO["acc_abs"][:], PRO["red1"][:]), mark=True)
            t0 = mm(bank(7, rows=1, w=1), onesf[:, 0:1], PRO["acc_abs"][:],
                    True, True, wait=[ptoks[nsup - 1]], mark=True)
            p.barrier()
            N = nsup * 128 * BW
            vop((lambda n: (lambda E: E.tensor_scalar(
                PRO["red1"][0:1, 0:1], bank(7, rows=1, w=1), 1.0 / n, 1e-5,
                ALU.mult, ALU.max)))(N))
            vop(lambda E: E.reciprocal(PRO["scal"][0:1, 0:1], PRO["red1"][0:1, 0:1]))
            v1 = vop(lambda E: E.tensor_copy(PRO["scal"][0:1, 1:2],
                                             PRO["red1"][0:1, 0:1]), mark=True)
            t1 = mm(bank(7, rows=128, w=2), onesf[0:1, :], PRO["scal"][0:1, 0:2],
                    True, True, wait=[v1], mark=True)
            p.barrier()
            vop(lambda E: E.tensor_copy(PRO["scal"][:], bank(7, rows=128, w=2)))
            p.barrier()
            sttoks = {}
            mmtoks = {}
            for b in range(nsup):
                w8 = [sttoks.get(b - 2)]
                tok = dma("sync", ldview(b % 2), src_fn(b), wait=w8)
                w = bufs[b % 2][:, 0:BW]
                vop((lambda w: (lambda E: E.tensor_scalar(
                    PRO["wscr"][:, 0:BW], w, PRO["scal"][:, 0:1], None,
                    ALU.mult)))(w),
                    wait=[tok, sttoks.get(b - 1), mmtoks.get(b - 1)])
                vop(lambda E: E.tensor_scalar(PRO["wi32"].tensor.ap()[:, 0:BW]
                                              if False else PRO["wtp"][:, 0:BW],
                                              PRO["wscr"][:, 0:BW], 0.5, None,
                                              ALU.is_ge))
                vop(lambda E: E.tensor_scalar(PRO["wscr"][:, 0:BW],
                                              PRO["wscr"][:, 0:BW], -0.5, None,
                                              ALU.is_le))
                vop(lambda E: E.tensor_sub(PRO["wscr"][:, 0:BW],
                                           PRO["wtp"][:, 0:BW],
                                           PRO["wscr"][:, 0:BW]))
                vop(lambda E: E.tensor_scalar(
                    PRO["wqbf"][:, 0:BW], PRO["wscr"][:, 0:BW], PRO["scal"][:, 1:2],
                    None, ALU.mult))
                for sb in range(nsub):
                    gi = b * nsub + sb
                    vop((lambda sb, gi: (lambda E: E.tensor_scalar(
                        PRO["wgbf"][:, sb * bw:(sb + 1) * bw],
                        PRO["wqbf"][:, sb * bw:(sb + 1) * bw],
                        g_colf[:, gi:gi + 1], None, ALU.mult)))(sb, gi),
                        mark=(sb == nsub - 1))
                vtok = ("Ev", p.cnt["Ev"])
                for sb in range(nsub):
                    gi = b * nsub + sb
                    for ch in range(bw // 512):
                        mm(bank(csbanks[ch], rows=2, w=512),
                           bvec_col[:, 2 * gi: 2 * gi + 2],
                           PRO["wqbf"][:, sb * bw + ch * 512: sb * bw + (ch + 1) * 512],
                           gi == 0, gi == nsup * nsub - 1, wait=[vtok],
                           mark=(sb == nsub - 1 and ch == bw // 512 - 1))
                mmtoks[b] = ("Et", p.cnt["Et"])
                src_st = (PRO["wgbf"][:, 0:BW] if nsub == 1 else
                          PRO["wgbf"][:, 0:BW].rearrange("p (c w) -> p c w",
                                                         c=nsub))
                sttoks[b] = dma("sync", dst_fn(b), src_st, wait=[vtok])
            p.barrier()

        quant(w1t, 8, 1, HID,
              lambda b: w1g_d[b],
              lambda b: w1t[b * 128:(b + 1) * 128, :],
              b1g1_bf, g1_colf, list(range(8)))
        if PRO_LEVEL < 3:
            return
        sttok = None
        for ch in range(8):
            vt = vop((lambda ch: (lambda E: E.tensor_copy(
                PRO["stage"][0:2, :], bank(ch, rows=2, w=512))))(ch),
                wait=[sttok], mark=True)
            d1 = dma("sync", cs_d[ch * 512:(ch + 1) * 512], PRO["stage"][0:1, :],
                     wait=[vt])
            sttok = dma("sync", cs2_d[ch * 512:(ch + 1) * 512],
                        PRO["stage"][1:2, :], wait=[vt])
        dma("sync", c1_col[:], cs_d[0:HID].rearrange("(jj pp) -> pp jj", pp=128),
            wait=[sttok], slow=True)
        dma("sync", PRO["b1f_col"][:], b1f_in.rearrange("(jj pp) -> pp jj", pp=128),
            slow=True)
        dma("gpsimd", s1_row[0:1, :], cs2_d[0:HID], wait=[sttok])
        p.barrier()
        vop(lambda E: E.tensor_add(c1_col[:], c1_col[:], PRO["b1f_col"][:]))
        p.barrier()
        if PRO_LEVEL < 4:
            return

        quant(w2t, 8, 4, D,
              lambda b: w2g_d[b * 4:(b + 1) * 4].rearrange("c p w -> p c w"),
              lambda b: w2t[b * 512:(b + 1) * 512, :]
              .rearrange("(c p) w -> p c w", p=128),
              b2g2_bf, g2_colf, [0, 1])
        sttok = None
        for ch in range(2):
            vt = vop((lambda ch: (lambda E: E.tensor_copy(
                PRO["stage"][0:2, :], bank(ch, rows=2, w=512))))(ch),
                wait=[sttok], mark=True)
            d1 = dma("sync", cs_d[ch * 512:(ch + 1) * 512], PRO["stage"][0:1, :],
                     wait=[vt])
            sttok = dma("sync", cs2_d[ch * 512:(ch + 1) * 512],
                        PRO["stage"][1:2, :], wait=[vt])
        dma("sync", c2_col[:], cs_d[0:D].rearrange("(jj pp) -> pp jj", pp=128),
            wait=[sttok], slow=True)
        dma("sync", PRO["b2f_col"][:], b2f_in.rearrange("(jj pp) -> pp jj", pp=128),
            slow=True)
        dma("gpsimd", s2_row[0:1, :], cs2_d[0:D], wait=[sttok])
        p.barrier()
        vop(lambda E: E.tensor_add(c2_col[:], c2_col[:], PRO["b2f_col"][:]))
        p.barrier()
        if PRO_LEVEL < 5:
            return

        for (win, wdst, wsumrow) in ((wq_in, wq_d, wqsum), (wk_in, wk_d, wksum),
                                     (wv_in, wv_d, wvsum), (wot_in, wo_d, None)):
            wtl = win.rearrange("(n p) w -> n p w", p=128)
            sttok = None
            mmtok = None
            for b in range(8):
                tok = dma("gpsimd", PRO["wqbf"][:, 0:D], wtl[b],
                          wait=[sttok, mmtok])
                sttok = dma("sync", wdst[b], PRO["wqbf"][:, 0:D], wait=[tok])
                if wsumrow is not None:
                    for ch in range(2):
                        mm(bank(ch, rows=128, w=512), ones_bf[:],
                           PRO["wqbf"][:, ch * 512:(ch + 1) * 512],
                           b == 0, b == 7, wait=[tok], mark=(ch == 1))
                    mmtok = ("Et", p.cnt["Et"])
            p.barrier()
            if wsumrow is not None:
                for ch in range(2):
                    vop(lambda E, ch=ch, w=wsumrow: E.tensor_copy(
                        w[0:1, ch * 512:(ch + 1) * 512],
                        bank(ch, rows=1, w=512)))
            p.barrier()

    def emit_stats(s, ffn, want_full_nm=False):
        def stub(wait=()):
            vop(lambda E: E.memset(r_bc[:], 1.0), wait=wait)
            vop(lambda E: E.memset(nmur_row[0:1, :], 0.0))
            if want_full_nm:
                vop(lambda E: E.memset(ATA["nmur_f"][:], 0.0))
            p.barrier()
        if STATS_LEVEL < 1:
            stub()
            return
        xb = [xb_s[:, i * T:(i + 1) * T] for i in range(2)]
        x2 = [x2_s[:, i * T:(i + 1) * T] for i in range(2)]
        ttoks = [None, None]
        for dc in range(8):
            b = dc % 2
            xt = xres[:, dc * T:(dc + 1) * T]
            gt = gop(lambda E, xb=xb[b], xt=xt: E.tensor_copy(xb, xt),
                     wait=[ttoks[b]], mark=True)
            vt = vop(lambda E, xb=xb[b], x2=x2[b]: E.tensor_mul(x2, xb, xb),
                     wait=[gt], mark=True)
            for tc in range(4):
                mm(bank(tc), ones_bf[:], xb[b][:, tc * 512:(tc + 1) * 512],
                   dc == 0, dc == 7, wait=[gt])
                mm(bank(4 + tc), ones_bf[:], x2[b][:, tc * 512:(tc + 1) * 512],
                   dc == 0, dc == 7, wait=[vt], mark=(tc == 3))
            ttoks[b] = ("Et", p.cnt["Et"])
        sumtok = ("Et", p.cnt["Et"])
        SUM = PS[:, 0:2048]
        SQ = PS[:, 2048:4096]
        if STATS_LEVEL < 2:
            stub(wait=[sumtok])
            return
        for c4 in range(4):
            sl = slice(c4 * 512, (c4 + 1) * 512)
            vop((lambda sl: (lambda E: E.tensor_scalar(
                t_mu[:, sl], SUM[:, sl], 1.0 / D, None, ALU.mult)))(sl),
                wait=[sumtok] if c4 == 0 else ())
        vop(lambda E: E.tensor_mul(t_x[:], t_mu[:], t_mu[:]))
        v1 = None
        for c4 in range(4):
            sl = slice(c4 * 512, (c4 + 1) * 512)
            v1 = vop((lambda sl: (lambda E: E.scalar_tensor_tensor(
                t_v[:, sl], SQ[:, sl], 1.0 / D, t_x[:, sl], ALU.mult,
                ALU.subtract)))(sl), mark=(c4 == 3))
        if STATS_LEVEL < 3:
            stub(wait=[v1])
            return
        sop(lambda E: E.activation(t_x[:], t_v[:], AF.Ln, bias=EPS), wait=[v1])
        s1t = sop(lambda E: E.activation(r_bc[:], t_x[:], AF.Exp, scale=-0.5),
                  mark=True)
        if want_full_nm:
            vop(lambda E: E.scalar_tensor_tensor(
                ATA["nmur_f"][:], t_mu[:], -1.0, r_bc[:], ALU.mult, ALU.mult),
                wait=[s1t])
            vop(lambda E: E.tensor_copy(nmur_row[0:1, :], ATA["nmur_f"][0:1, :]))
        else:
            vop(lambda E: E.scalar_tensor_tensor(
                t_mu[0:1, :], t_mu[0:1, :], -1.0, r_bc[0:1, :], ALU.mult,
                ALU.mult), wait=[s1t])
        if ffn:
            sq2 = sop(lambda E: E.activation(t_x[:], t_x[:], AF.Exp, scale=-1.0),
                      mark=True)
            v2 = vop(lambda E: E.tensor_mul(t_v[:], t_v[:], t_x[:]), wait=[sq2],
                     mark=True)
            sop(lambda E: E.activation(t_x[:], t_v[:], AF.Ln, bias=EPS), wait=[v2])
            s2t = sop(lambda E: E.activation(t_x[:], t_x[:], AF.Exp, scale=-0.5),
                      mark=True)
            vop(lambda E: E.tensor_mul(t_mu[0:1, :], t_mu[0:1, :], t_x[0:1, :]),
                wait=[s2t])
            vop(lambda E: E.tensor_mul(r_bc[:], r_bc[:], t_x[:]))
        if not want_full_nm:
            v9 = vop(lambda E: E.tensor_copy(nmur_row[0:1, :], t_mu[0:1, :]),
                     mark=True)
            if DBG and ffn and s == 0:
                dma("sync", dbg_r, r_bc[:], wait=[v9])
        p.barrier()

    def emit_load_x(s):
        for dc in range(8):
            dma("sync", xres[:, dc * T:(dc + 1) * T],
                x_in[s, dc * 128:(dc + 1) * 128, :])
        p.barrier()

    def emit_store_x(s):
        if DBG and s == 0:
            for dc in range(8):
                dma("sync", dbg2[dc], xres[:, dc * T: dc * T + 512])
        for dc in range(8):
            dma("sync", y_out[s, dc * 128:(dc + 1) * 128, :],
                xres[:, dc * T:(dc + 1) * T])
        p.barrier()

    def emit_attn_a(s):
        gop(lambda E: E.memset(ATA["pa"][:], 0.0))
        for dc in range(8):
            gop(lambda E, dc=dc: E.memset(ATA["xpad"][:, dc * PADW: dc * PADW + 4], 0.0))
            gop(lambda E, dc=dc: E.memset(
                ATA["xpad"][:, dc * PADW + 4 + T: (dc + 1) * PADW], 0.0))
            gop(lambda E, dc=dc: E.memset(ATA["xpad1"][:, dc * PADW: dc * PADW + 5], 0.0))
            gop(lambda E, dc=dc: E.memset(
                ATA["xpad1"][:, dc * PADW + 5 + T: (dc + 1) * PADW], 0.0))
        for dc in range(8):
            xt = xres[:, dc * T:(dc + 1) * T]
            vop(lambda E, xt=xt: E.tensor_mul(ATA["xlns"][:], xt, r_bc[:]))
            vt = vop(lambda E, dc=dc: E.tensor_add(
                ATA["xpad"][:, dc * PADW + 4: dc * PADW + 4 + T], ATA["xlns"][:],
                ATA["nmur_f"][:]), mark=True)
            gop(lambda E, dc=dc: E.tensor_copy(
                ATA["xpad1"][:, dc * PADW + 5: dc * PADW + 5 + T],
                ATA["xpad"][:, dc * PADW + 4: dc * PADW + 4 + T]), wait=[vt],
                mark=True)
        p.barrier()

        def xp(tau, dc, col, w):
            if tau % 2 == 0:
                return ATA["xpad"][:, dc * PADW + col + tau: dc * PADW + col + tau + w]
            return ATA["xpad1"][:, dc * PADW + col + tau + 1:
                               dc * PADW + col + tau + 1 + w]

        if ATA_LEVEL < 2:
            p.barrier()
            return
        prev_v = None
        for hpg in range(2):
            for tc in range(4):
                col = tc * 512
                for tau in range(K9):
                    for k in range(4):
                        hp = hpg * 4 + k
                        mm(PS[32 * k:32 * k + 18, k * 512:(k + 1) * 512],
                           cwl_bf[:, (tau * 8 + hp) * 18:(tau * 8 + hp + 1) * 18],
                           xp(tau, hp, col, 512), tau == 0, tau == K9 - 1,
                           tp=(0, 32 * k),
                           wait=([prev_v] if (tau == 0 and k == 0) else ()),
                           mark=(tau == K9 - 1 and k == 3))
                ctok = ("Et", p.cnt["Et"])
                for k in range(4):
                    sop(lambda E, k=k: E.activation(
                        ATA["pa"][32 * k:32 * k + 18, :],
                        PS[32 * k:32 * k + 18, k * 512:(k + 1) * 512], AF.Exp),
                        wait=[ctok], mark=(k == 3))
                patok = ("Es", p.cnt["Es"])
                if ATA_LEVEL < 3:
                    p.barrier()
                    prev_v = None
                    continue
                dtok = mm(bank(4, rows=8, w=512), g2full[:], ATA["pa"][:], True,
                          True, wait=[patok], mark=True)
                sop(lambda E: E.activation(ATA["rcp_f"][:], bank(4, rows=8, w=512),
                                           AF.Ln), wait=[dtok])
                sop(lambda E: E.activation(ATA["rcp_b"][:], ATA["rcp_f"][:],
                                           AF.Exp, scale=-1.0), mark=True)
                rcptok = ("Es", p.cnt["Es"])
                if ATA_LEVEL < 4:
                    p.barrier()
                    prev_v = None
                    continue
                vtoks = {}
                for tau in range(K9):
                    g = (tau % 2) * 4
                    for k in range(4):
                        w8 = []
                        if tau == 0 or tau == 1:
                            w8.append(patok if g == 0 else rcptok)
                        if tau >= 2:
                            w8.append(vtoks[(tau - 2, k)])
                        mm(PS[:, (g + k) * 512:(g + k + 1) * 512],
                           bsel2[32 * k:32 * k + 32, tau * 128:(tau + 1) * 128],
                           ATA["pa"][32 * k:32 * k + 32, :], True, True,
                           tp=(32 * k, 0), wait=w8, mark=(k == 3))
                    wtok = ("Et", p.cnt["Et"])
                    for k in range(4):
                        hp = hpg * 4 + k
                        accs = ATA["acc"][:, k * 512:(k + 1) * 512]
                        if tau == 0:
                            vtoks[(tau, k)] = vop(
                                (lambda a, xpap, psap: (lambda E: E.tensor_mul(
                                    a, xpap, psap)))(
                                    accs, xp(tau, hp, col, 512),
                                    PS[:, (g + k) * 512:(g + k + 1) * 512]),
                                wait=[wtok], mark=True)
                        else:
                            vop((lambda xpap, psap: (lambda E: E.tensor_mul(
                                ATA["prod"][:], xpap, psap)))(
                                xp(tau, hp, col, 512),
                                PS[:, (g + k) * 512:(g + k + 1) * 512]),
                                wait=[wtok])
                            vtoks[(tau, k)] = vop(
                                (lambda a: (lambda E: E.tensor_add(
                                    a, a, ATA["prod"][:])))(accs), mark=True)
                if ATA_LEVEL < 5:
                    p.barrier()
                    prev_v = ("Ev", p.cnt["Ev"])
                    continue
                for k in range(4):
                    hp = hpg * 4 + k
                    rb = mm(PS[:, (4 + k) * 512:(4 + k + 1) * 512],
                            brec[:, k * 128:(k + 1) * 128], ATA["rcp_b"][:],
                            True, True, wait=[rcptok, vtoks[(7, k)]], mark=True)
                    xt = xres[:, hp * T + col: hp * T + col + 512]
                    vop((lambda k: (lambda E: E.tensor_mul(
                        ATA["norm"][:], ATA["acc"][:, k * 512:(k + 1) * 512],
                        PS[:, (4 + k) * 512:(4 + k + 1) * 512])))(k),
                        wait=[rb, vtoks[(8, k)]])
                    vop((lambda xt: (lambda E: E.tensor_add(xt, xt,
                                                            ATA["norm"][:])))(xt),
                        mark=True)
                prev_v = ("Ev", p.cnt["Ev"])
        p.barrier()

    def emit_ffn(s, cut=None):
        cut = FFN_CUT if cut is None else cut
        wbufs = [FFN["wblkA"], FFN["wblkB"]]
        prev_w2 = None
        for tc in range(NTC):
            col = tc * TC
            if cut == 7 and tc == 2:
                p.barrier()
                break
            if cut == 8 and tc == 3:
                p.barrier()
                break
            if cut == 9 and tc < 3:
                continue
            if cut == 3 and tc == 1:
                p.barrier()
                break
            for dc in range(8):
                xt = xres[:, dc * T + col: dc * T + col + TC]
                vop((lambda xt, dc, rr: (lambda E: E.tensor_mul(
                    FFN["xs"][:, dc * TC:(dc + 1) * TC], xt, rr)))(
                    xt, dc, r_bc[:, col:col + TC]), mark=(dc == 7))
            xstok = ("Ev", p.cnt["Ev"])
            if DBG and s == 0 and tc == 0:
                dma("sync", dbg_xs, FFN["xs"][:], wait=[xstok])
            if cut == 4 and tc == 1:
                p.barrier()
                break
            dtoks = {0: dma("sync", wbufs[0][:].rearrange("p (c j) -> p c j", c=8),
                            w1g_d[:, :, 0:512].rearrange("c p j -> p c j"),
                            wait=[prev_w2])}
            if DBG and s == 0 and tc == 0:
                dma("sync", dbg_w, wbufs[0][:], wait=[dtoks[0]])
            getoks = {}
            p0tok = None
            pe_jb = {}
            if cut == 5 and tc == 1:
                p.barrier()
                break
            for jb in range(8):
                if jb + 1 < 8:
                    dtoks[jb + 1] = dma(
                        "sync",
                        wbufs[(jb + 1) % 2][:].rearrange("p (c j) -> p c j", c=8),
                        w1g_d[:, :, (jb + 1) * 512:(jb + 2) * 512].rearrange("c p j -> p c j"),
                        wait=[pe_jb.get(jb - 1), prev_w2])
                for jt in range(4):
                    jj = jb * 4 + jt
                    bk = jj % 2
                    w8 = [dtoks[jb], xstok, getoks.get(jj - 2)]
                    if jj == 2 and p0tok is not None:
                        w8.append(p0tok)
                    for dc in range(8):
                        mm(bank(bk),
                           wbufs[jb % 2][:, dc * 512 + jt * 128: dc * 512 + jt * 128 + 128],
                           FFN["xs"][:, dc * TC:(dc + 1) * TC],
                           dc == 0, False, wait=(w8 if dc == 0 else ()))
                    mm(bank(bk), s1_row[0:1, jj * 128:(jj + 1) * 128],
                       nmur_row[0:1, col:col + TC], False, True, mark=True)
                    mtok = ("Et", p.cnt["Et"])
                    if DBG and s == 0 and tc == 0 and jj == 0:
                        cpt = vop((lambda bk: (lambda E: E.tensor_copy(
                            FFN["ev"][:, 0:512], bank(bk))))(bk),
                            wait=[mtok], mark=True)
                        p0tok = dma("gpsimd", dbg_p0, FFN["ev"][:, 0:512],
                                    wait=[cpt])
                    sop((lambda jj, bk: (lambda E: E.activation(
                        FFN["h"][:, jj * TC:(jj + 1) * TC], bank(bk), AF.Gelu,
                        bias=c1_col[:, jj:jj + 1])))(jj, bk), wait=[mtok],
                        mark=True)
                    getoks[jj] = ("Es", p.cnt["Es"])
                pe_jb[jb] = mtok
            if DBG and s == 0 and tc == 0:
                dma("sync", dbg_h, FFN["h"][:], wait=[getoks[31]])
            if cut == 1 and tc == 1:
                p.barrier()
                break
            for jj in range(32):
                mm(bank(2), ones_bf[:], FFN["h"][:, jj * TC:(jj + 1) * TC],
                   jj == 0, jj == 31, wait=[getoks[jj]],
                   mark=(jj == 31))
            mutok = ("Et", p.cnt["Et"])
            v0 = vop(lambda E: E.tensor_scalar(FFN["t3a"][:], bank(2), 1.0 / HID,
                                               None, ALU.mult), wait=[mutok],
                     mark=True)
            sqtoks = {}
            for jj in range(32):
                b2 = jj % 2
                h_t = FFN["h"][:, jj * TC:(jj + 1) * TC]
                ct = FFN["h2"][:, (2 * b2) * TC:(2 * b2 + 1) * TC]
                sq = FFN["h2"][:, (2 * b2 + 1) * TC:(2 * b2 + 2) * TC]
                vop((lambda h_t, ct: (lambda E: E.tensor_sub(ct, h_t,
                                                             FFN["t3a"][:])))(
                    h_t, ct), wait=[v0, sqtoks.get(jj - 2)])
                vt = vop((lambda ct, sq: (lambda E: E.tensor_mul(sq, ct, ct)))(
                    ct, sq), mark=True)
                sqtoks[jj] = mm(bank(3), ones_bf[:], sq,
                                jj == 0, jj == 31, wait=[vt], mark=True)
            sumtok = sqtoks[31]
            v1 = vop(lambda E: E.tensor_scalar(FFN["t3b"][:], bank(3), 1.0 / HID,
                                               None, ALU.mult), wait=[sumtok],
                     mark=True)
            sop(lambda E: E.activation(FFN["t3b"][:], FFN["t3b"][:], AF.Ln,
                                       bias=EPS), wait=[v1])
            s1t = sop(lambda E: E.activation(r3_bc[:], FFN["t3b"][:], AF.Exp,
                                             scale=-0.5), mark=True)
            vop(lambda E: E.tensor_scalar_mul(
                FFN["t3a"][0:1, :], FFN["t3a"][0:1, :], -1.0), wait=[s1t])
            v2 = vop(lambda E: E.tensor_copy(nm3_row[0:1, :], FFN["t3a"][0:1, :]),
                     mark=True)
            nm3tok = v2
            if cut == 2 and tc == 1:
                p.barrier()
                break
            dtoks = {0: dma("sync", wbufs[0][:].rearrange("p (c j) -> p c j", c=4),
                            w2g_d[0:4, :, :].rearrange("c p j -> p c j"), wait=[pe_jb[7]])}
            petoks = {}
            for jcb in range(8):
                if jcb + 1 < 8:
                    dtoks[jcb + 1] = dma(
                        "sync",
                        wbufs[(jcb + 1) % 2][:].rearrange("p (c j) -> p c j", c=4),
                        w2g_d[(jcb + 1) * 4:(jcb + 2) * 4, :, :].rearrange("c p j -> p c j"),
                        wait=[petoks.get(jcb - 1)])
                for et in range(8):
                    for jcl in range(4):
                        w8 = [dtoks[jcb], nm3tok, getoks[31]] \
                            if (et == 0 and jcl == 0) else ()
                        mm(bank(et),
                           wbufs[jcb % 2][:, jcl * 1024 + et * 128: jcl * 1024 + et * 128 + 128],
                           FFN["h"][:, (jcb * 4 + jcl) * TC:(jcb * 4 + jcl + 1) * TC],
                           jcb == 0 and jcl == 0, False, wait=w8,
                           mark=(et == 7 and jcl == 3))
                petoks[jcb] = ("Et", p.cnt["Et"])
            prev_w2 = petoks[7]
            restoks = []
            for et in range(8):
                mm(bank(et), s2_row[0:1, et * 128:(et + 1) * 128],
                   nm3_row[0:1, :], False, True, mark=True)
                mtok = ("Et", p.cnt["Et"])
                ev = FFN["ev"][:, (et % 2) * TC:((et % 2) + 1) * TC]
                xt = xres[:, et * T + col: et * T + col + TC]
                vop((lambda et, ev: (lambda E: E.scalar_tensor_tensor(
                    ev, bank(et), 1.0, r3_bc[:], ALU.mult, ALU.mult)))(et, ev),
                    wait=[mtok, (p0tok if et == 0 else None)])
                restoks.append(vop((lambda et, ev, xt: (lambda E: E.scalar_tensor_tensor(
                    xt, ev, c2_col[:, et:et + 1], xt, ALU.add, ALU.add)))(
                    et, ev, xt), mark=True))
            if s == 0 and tc == 0 and DBG:
                for dc in range(8):
                    dma("sync", dbg_out[dc],
                        xres[:, dc * T: dc * T + 512], wait=[restoks[dc]])
            p.barrier()

    def emit_attn_b(s):
        vt = ATB["v"]
        vop(lambda E: E.memset(vt[:], 0.0))
        vop(lambda E: E.memset(ATB["dens"][:], 1.0))
        vop(lambda E: E.memset(ATB["rcpb"][:], 0.0), mark=True)
        vinit = ("Ev", p.cnt["Ev"])
        for tt in range(NKT):
            for hp in range(8):
                base = tt * 8 * VST + hp * VST + 64 + 2 * hp
                gop((lambda b: (lambda E: E.memset(vt[:, b:b + 2], 1.0)))(base),
                    wait=[vinit] if (tt == 0 and hp == 0) else ())
        p.barrier()

        wbufs = [ATB["wbufA"], ATB["wbufB"]]
        prev_v = None
        for tc in range(4):
            col = tc * 512
            for dc in range(8):
                xt = xres[:, dc * T + col: dc * T + col + 512]
                vop((lambda xt, dc, rr: (lambda E: E.tensor_mul(
                    ATB["xs5"][:, dc * 512:(dc + 1) * 512], xt, rr)))(
                    xt, dc, r_bc[:, col:col + 512]), mark=(dc == 7))
            xstok = ("Ev", p.cnt["Ev"])
            ktoks = {}
            dtoks = {0: dma("sync", wbufs[0][:].rearrange("p (c j) -> p c j", c=2),
                            wk_d[0:2].rearrange("c p j -> p c j"), wait=[prev_v])}
            for wp in range(4):
                if wp + 1 < 4:
                    dtoks[wp + 1] = dma(
                        "sync",
                        wbufs[(wp + 1) % 2][:].rearrange("p (c j) -> p c j", c=2),
                        wk_d[2 * (wp + 1): 2 * (wp + 2)].rearrange("c p j -> p c j"),
                        wait=[ktoks.get(wp - 1)])
                for dcl in range(2):
                    dc = 2 * wp + dcl
                    for fc in range(8):
                        mm(bank(fc),
                           wbufs[wp % 2][:, dcl * 1024 + fc * 128: dcl * 1024 + fc * 128 + 128],
                           ATB["xs5"][:, dc * 512:(dc + 1) * 512],
                           wp == 0 and dcl == 0, False,
                           wait=([dtoks[wp], xstok] if (dcl == 0 and fc == 0) else ()),
                           mark=(dcl == 1 and fc == 7))
                ktoks[wp] = ("Et", p.cnt["Et"])
            kev = {}
            for fc in range(8):
                mm(bank(fc), wksum[0:1, fc * 128:(fc + 1) * 128],
                   nmur_row[0:1, col:col + 512], False, True, mark=True)
                mt = ("Et", p.cnt["Et"])
                kev[fc] = vop((lambda dst, fc: (lambda E: E.tensor_copy(
                    dst, bank(fc))))(
                    ATB["K"][:, fc * T + col: fc * T + col + 512], fc),
                    wait=[mt], mark=True)
            vtoks = {}
            dtoks = {0: dma("sync", wbufs[0][:].rearrange("p (c j) -> p c j", c=2),
                            wv_d[0:2].rearrange("c p j -> p c j"), wait=[ktoks[3]])}
            for wp in range(4):
                if wp + 1 < 4:
                    dtoks[wp + 1] = dma(
                        "sync",
                        wbufs[(wp + 1) % 2][:].rearrange("p (c j) -> p c j", c=2),
                        wv_d[2 * (wp + 1): 2 * (wp + 2)].rearrange("c p j -> p c j"),
                        wait=[vtoks.get(wp - 1)])
                for dcl in range(2):
                    dc = 2 * wp + dcl
                    for tt in range(4):
                        for nch in range(2):
                            b = 2 * tt + nch
                            w8 = [dtoks[wp], kev[b]] if (dcl == 0 and tt == 0
                                                         and nch == 0) else ()
                            if wp == 0 and dcl == 0:
                                w8 = [dtoks[wp], kev[b]]
                            mm(bank(b),
                               ATB["xs5"][:, dc * 512 + tt * 128: dc * 512 + tt * 128 + 128],
                               wbufs[wp % 2][:, dcl * 1024 + nch * 512: dcl * 1024 + (nch + 1) * 512],
                               wp == 0 and dcl == 0, False, wait=w8,
                               mark=(dcl == 1 and tt == 3 and nch == 1))
                vtoks[wp] = ("Et", p.cnt["Et"])
            for tt in range(4):
                ttg = tc * 4 + tt
                for nch in range(2):
                    mm(bank(2 * tt + nch),
                       nmur_row[0:1, col + tt * 128: col + tt * 128 + 128],
                       wvsum[0:1, nch * 512:(nch + 1) * 512], False, True,
                       mark=(nch == 1))
                mt = ("Et", p.cnt["Et"])
                for nch in range(2):
                    dst = bass.AP(vt.tensor, ttg * 8 * VST + nch * 4 * VST,
                                  [[NKT * 8 * VST, 128], [VST, 4], [96, 2], [1, 64]])
                    src = bass.AP(PS.tensor, (2 * tt + nch) * 512,
                                  [[4096, 128], [128, 4], [64, 2], [1, 64]])
                    vop((lambda d, sr: (lambda E: E.tensor_copy(d, sr)))(dst, src),
                        wait=[mt], mark=(nch == 1))
            prev_v = vtoks[3]
        p.barrier()

        res_tok = None
        ptw = {0: [], 1: []}
        prev_wo_tok = None
        for qc in range(NQC):
            col = qc * QC
            nkt_q = 2 * qc + 2
            for dc in range(8):
                xt = xres[:, dc * T + col: dc * T + col + QC]
                vop((lambda xt, dc, rr: (lambda E: E.tensor_mul(
                    ATB["xs5"][:, dc * 512: dc * 512 + QC], xt, rr)))(
                    xt, dc, r_bc[:, col:col + QC]),
                    wait=([res_tok] if dc == 0 else ()), mark=(dc == 7))
            xstok = ("Ev", p.cnt["Ev"])
            qtoks = {}
            dtoks = {0: dma("sync", wbufs[0][:].rearrange("p (c j) -> p c j", c=2),
                            wq_d[0:2].rearrange("c p j -> p c j"), wait=[prev_wo_tok])}
            for wp in range(4):
                if wp + 1 < 4:
                    dtoks[wp + 1] = dma(
                        "sync",
                        wbufs[(wp + 1) % 2][:].rearrange("p (c j) -> p c j", c=2),
                        wq_d[2 * (wp + 1): 2 * (wp + 2)].rearrange("c p j -> p c j"),
                        wait=[qtoks.get(wp - 1)])
                for dcl in range(2):
                    dc = 2 * wp + dcl
                    for fc in range(8):
                        mm(bank(fc, w=QC),
                           wbufs[wp % 2][:, dcl * 1024 + fc * 128: dcl * 1024 + fc * 128 + 128],
                           ATB["xs5"][:, dc * 512: dc * 512 + QC],
                           wp == 0 and dcl == 0, False,
                           wait=([dtoks[wp], xstok] if (dcl == 0 and fc == 0) else ()),
                           mark=(dcl == 1 and fc == 7))
                qtoks[wp] = ("Et", p.cnt["Et"])
            qev = {}
            for fc in range(8):
                mm(bank(fc, w=QC), wqsum[0:1, fc * 128:(fc + 1) * 128],
                   nmur_row[0:1, col:col + QC], False, True, mark=True)
                mt = ("Et", p.cnt["Et"])
                qev[fc] = vop((lambda fc: (lambda E: E.tensor_copy(
                    ATB["Q"][:, fc * QC:(fc + 1) * QC], bank(fc, w=QC))))(fc),
                    wait=[mt], mark=True)
            otoks = {}
            denstok = None
            for hp in range(8):
                pts = (ATB["ptA"], ATB["ptB"])
                nb = (nkt_q + 3) // 4
                pvdone = {0: otoks.get(hp - 1), 1: otoks.get(hp - 1)}
                expdone = {}
                for b in range(nb):
                    kts = list(range(4 * b, min(4 * b + 4, nkt_q)))
                    for i, kt in enumerate(kts):
                        for eta in (0, 1):
                            w8 = []
                            if i == 0:
                                w8 = [qev[7], otoks.get(hp - 1)] + \
                                    expdone.get((b - 1, eta), [])
                            mm(PS[:, eta * 1024 + i * QC: eta * 1024 + (i + 1) * QC],
                               ATB["K"][eta * 64:(eta + 1) * 64,
                                        hp * T + kt * 128: hp * T + (kt + 1) * 128],
                               ATB["Q"][eta * 64:(eta + 1) * 64,
                                        hp * QC:(hp + 1) * QC],
                               True, True, tp=(eta * 64, 0),
                               wait=w8, mark=(i == len(kts) - 1))
                    sctok = ("Et", p.cnt["Et"])
                    for eta in (0, 1):
                        pt = pts[eta]
                        wrt = [sctok, pvdone[eta]] + ptw[eta]
                        estoks = []
                        othtoks = []
                        pref = [kt for kt in kts if kt < 2 * qc]
                        if pref:
                            i0 = kts.index(pref[0])
                            w = len(pref) * QC
                            sop((lambda eta, i0, w, pt: (lambda E: E.activation(
                                pt[:, i0 * QC: i0 * QC + w],
                                PS[:, eta * 1024 + i0 * QC: eta * 1024 + i0 * QC + w],
                                AF.Exp, scale=0.125)))(eta, i0, w, pt),
                                wait=wrt, mark=True)
                            estoks.append(("Es", p.cnt["Es"]))
                        for kt in kts:
                            if kt < 2 * qc:
                                continue
                            i = kts.index(kt)
                            if kt == 2 * qc:
                                sop((lambda eta, i, pt: (lambda E: E.activation(
                                    pt[:, i * QC:(i + 1) * QC],
                                    PS[:, eta * 1024 + i * QC: eta * 1024 + (i + 1) * QC],
                                    AF.Exp, scale=0.125)))(eta, i, pt),
                                    wait=wrt, mark=True)
                                st = ("Es", p.cnt["Es"])
                                estoks.append(st)
                                gop((lambda pt, i: (lambda E: E.affine_select(
                                    pt[:, i * QC: i * QC + 128],
                                    pt[:, i * QC: i * QC + 128],
                                    [[1, 128]], ALU.is_ge, 0.0, base=0,
                                    channel_multiplier=-1)))(pt, i),
                                    wait=[st] + wrt, mark=True)
                                othtoks.append(("Eg", p.cnt["Eg"]))
                            else:
                                sop((lambda eta, i, pt: (lambda E: E.activation(
                                    pt[:, i * QC + 128:(i + 1) * QC],
                                    PS[:, eta * 1024 + i * QC + 128:
                                       eta * 1024 + (i + 1) * QC],
                                    AF.Exp, scale=0.125)))(eta, i, pt),
                                    wait=wrt, mark=True)
                                st = ("Es", p.cnt["Es"])
                                estoks.append(st)
                                vop((lambda pt, i: (lambda E: E.memset(
                                    pt[:, i * QC: i * QC + 128], 0.0)))(pt, i),
                                    wait=wrt, mark=True)
                                othtoks.append(("Ev", p.cnt["Ev"]))
                                gop((lambda pt, i: (lambda E: E.affine_select(
                                    pt[:, i * QC + 128:(i + 1) * QC],
                                    pt[:, i * QC + 128:(i + 1) * QC],
                                    [[1, 128]], ALU.is_ge, 0.0, base=0,
                                    channel_multiplier=-1)))(pt, i),
                                    wait=[st] + wrt, mark=True)
                                othtoks.append(("Eg", p.cnt["Eg"]))
                        mtoks = estoks + othtoks
                        ptw[eta] = mtoks
                        expdone[(b, eta)] = estoks
                        h = 2 * hp + eta
                        ob = 4 + eta
                        for i, kt in enumerate(kts):
                            if eta == 0:
                                lhs = vt[:, kt * 8 * VST + hp * VST:
                                         kt * 8 * VST + hp * VST + 96]
                                rows = 96
                            else:
                                lhs = vt[:, kt * 8 * VST + hp * VST + 32:
                                         kt * 8 * VST + hp * VST + 160]
                                rows = 128
                            mm(PS[0:rows, ob * 512: ob * 512 + QC], lhs,
                               pt[:, i * QC:(i + 1) * QC],
                               kt == 0, kt == nkt_q - 1,
                               wait=mtoks, mark=(kt == nkt_q - 1))
                        if 4 * b + 4 >= nkt_q:
                            pvdone[eta] = ("Et", p.cnt["Et"])
                        else:
                            pvdone[eta] = ("Et", p.cnt["Et"])
                for eta in (0, 1):
                    ob = 4 + eta
                    drow = 64 + 2 * hp if eta == 0 else 33 + 2 * hp
                    st_ranges = ((0, 96),) if eta == 0 else ((32, 64), (64, 128))
                    sttok = None
                    for a, b2 in st_ranges:
                        sttok = vop((lambda ob, a, b2: (lambda E: E.tensor_copy(
                            ATB["ostg"][a:b2, :],
                            PS[a:b2, ob * 512: ob * 512 + QC])))(ob, a, b2),
                            wait=[pvdone[eta], denstok], mark=True)
                    denstok = dma("sync", ATB["dens"][drow:drow + 1, :],
                                  ATB["ostg"][drow:drow + 1, :], wait=[sttok])
                    vop((lambda eta, hp: (lambda E: E.tensor_copy(
                        ATB["ao"][eta * 64:(eta + 1) * 64,
                                  hp * QC:(hp + 1) * QC],
                        ATB["ostg"][eta * 64:(eta + 1) * 64, :])))(eta, hp),
                        mark=(eta == 1))
                otoks[hp] = ("Ev", p.cnt["Ev"])
            for a, b2 in ((32, 64), (64, 80)):
                sop((lambda a, b2: (lambda E: E.activation(
                    ATB["rcpf"][a:b2, :], ATB["dens"][a:b2, :], AF.Ln)))(a, b2),
                    wait=[otoks[7], denstok])
                sop((lambda a, b2: (lambda E: E.activation(
                    ATB["rcpb"][a:b2, :], ATB["rcpf"][a:b2, :], AF.Exp,
                    scale=-1.0)))(a, b2), mark=True)
            rcptok = ("Es", p.cnt["Es"])
            naotoks = {}
            for fc in range(8):
                w8 = [rcptok, otoks[fc] if False else otoks[7]]
                if fc >= 1:
                    w8.append(naotoks[fc - 1])
                rb = mm(bank(6, w=QC), selfc[0:80, fc * 128:(fc + 1) * 128],
                        ATB["rcpb"][0:80, :], True, True, wait=w8, mark=True)
                naotoks[fc] = vop((lambda fc: (lambda E: E.tensor_mul(
                    ATB["ao"][:, fc * QC:(fc + 1) * QC],
                    ATB["ao"][:, fc * QC:(fc + 1) * QC], bank(6, w=QC))))(fc),
                    wait=[rb], mark=True)
            wtoks = {}
            dtoks = {0: dma("sync", wbufs[0][:].rearrange("p (c j) -> p c j", c=2),
                            wo_d[0:2].rearrange("c p j -> p c j"), wait=[qtoks[3]])}
            for wp in range(4):
                if wp + 1 < 4:
                    dtoks[wp + 1] = dma(
                        "sync",
                        wbufs[(wp + 1) % 2][:].rearrange("p (c j) -> p c j", c=2),
                        wo_d[2 * (wp + 1): 2 * (wp + 2)].rearrange("c p j -> p c j"),
                        wait=[wtoks.get(wp - 1)])
                for dcl in range(2):
                    fc = 2 * wp + dcl
                    for et in range(8):
                        w8 = [dtoks[wp], naotoks[7]] if et == 0 else ()
                        mm(bank(et, w=QC),
                           wbufs[wp % 2][:, dcl * 1024 + et * 128: dcl * 1024 + et * 128 + 128],
                           ATB["ao"][:, fc * QC:(fc + 1) * QC],
                           fc == 0, fc == 7, wait=w8,
                           mark=(fc == 7 or (dcl == 1 and et == 7)))
                wtoks[wp] = ("Et", p.cnt["Et"])
            wotok = ("Et", p.cnt["Et"])
            prev_wo_tok = wtoks[3]
            for et in range(8):
                xt = xres[:, et * T + col: et * T + col + QC]
                vop((lambda et, xt: (lambda E: E.tensor_add(
                    xt, xt, bank(et, w=QC))))(et, xt), wait=[wotok],
                    mark=(et == 7))
            res_tok = ("Ev", p.cnt["Ev"])
        p.barrier()

    if N_SUB >= 0:
        emit_prologue()
    else:
        p.barrier()
    for s in range(SEQS):
        emit_load_x(s)
        if N_SUB >= 1:
            emit_stats(s, ffn=False, want_full_nm=True)
            emit_attn_a(s)
        if N_SUB >= 2:
            emit_stats(s, ffn=True)
            emit_ffn(s)
        if N_SUB >= 3:
            emit_stats(s, ffn=False)
            emit_attn_b(s)
        if N_SUB >= 4:
            emit_stats(s, ffn=True)
            emit_ffn(s, cut=FFN2_CUT)
        emit_store_x(s)

    p.replay_all()
    ctx.close()
    return nc


def kernel(**inputs):
    x = np.asarray(inputs["L0_attn_a_query"])
    cw = np.asarray(inputs["L0_attn_a_conv_weight"])
    wq = np.asarray(inputs["L0_attn_b_w_q"]).reshape(D, D)
    wk = np.asarray(inputs["L0_attn_b_w_k"]).reshape(D, D)
    wv = np.asarray(inputs["L0_attn_b_w_v"]).reshape(D, D)
    wo = np.asarray(inputs["L0_attn_b_w_o"]).reshape(D, D)
    consts = _host_consts(cw)
    shared = {
        "w1t": np.ascontiguousarray(np.asarray(inputs["L0_ffn_w1"]).T),
        "w2t": np.ascontiguousarray(np.asarray(inputs["L0_ffn_w2"]).T),
        "wq_in": np.ascontiguousarray(wq), "wk_in": np.ascontiguousarray(wk),
        "wv_in": np.ascontiguousarray(wv),
        "wot_in": np.ascontiguousarray(wo.T),
        "cwl": consts["cwl"], "onesf": consts["onesf"],
        "bsel2": consts["bsel2"], "g2full": consts["g2full"],
        "brec": consts["brec"], "selfc": consts["selfc"],
        "ln1g": np.asarray(inputs["L0_ffn_ln1_g"]),
        "ln1b": np.asarray(inputs["L0_ffn_ln1_b"]),
        "b1f": np.asarray(inputs["L0_ffn_b1"]),
        "ln2g": np.asarray(inputs["L0_ffn_ln2_g"]),
        "ln2b": np.asarray(inputs["L0_ffn_ln2_b"]),
        "b2f": np.asarray(inputs["L0_ffn_b2"]),
    }
    xt = np.ascontiguousarray(x.transpose(0, 2, 1))
    nc = build_nc()
    in_maps = []
    for c in range(N_CORES):
        m = dict(shared)
        m["x_in"] = np.ascontiguousarray(xt[2 * c: 2 * c + 2])
        in_maps.append(m)
    res = run_bass_kernel_spmd(nc, in_maps, core_ids=list(range(N_CORES)))
    y = np.stack([r["y"] for r in res.results], 0).reshape(16, D, T)
    return np.ascontiguousarray(y.transpose(0, 2, 1))
